# version 31 (final)
# baseline (speedup 1.0000x reference)
import math
import numpy as np
import ml_dtypes
import concourse.bass as bass
import concourse.mybir as mybir
from concourse.bass_utils import run_bass_kernel_spmd

F32 = mybir.dt.float32
BF16 = mybir.dt.bfloat16
I32 = mybir.dt.int32
AF = mybir.ActivationFunctionType
ALU = mybir.AluOpType
AX = mybir.AxisListType

NCORES = 8
D = 1024
E = 2048
H = 16
HD = 128
CAP = 100
NEXP = 8

TWO_PI = 2.0 * math.pi
CW_C1 = 6.28125
CW_C2 = float(np.float32(np.float32(TWO_PI) - np.float32(CW_C1)))
INV_2PI = float(np.float32(1.0 / TWO_PI))
RND_C = float(np.float32(1.5 * 2 ** 23))
CLAMP_HI = float(np.float32(math.pi / 2) - 3 * np.float32(2.0 ** -22))
CLAMP_LO = float(-np.float32(1.5 * math.pi) + 6 * np.float32(2.0 ** -22))
MAGIC = 0x5F3759DF
EPS_LN = 1e-5


class Emit:

    def __init__(self):
        self.tick = {}
        self.qcnt = {}
        self.last_w = {}
        self.readers = {}
        self.waited = {}
        self.pending = {}

    def op(self, sect, meth, *args, reads=(), writes=(), queue=None, **kwargs):
        src = queue if queue is not None else sect
        waits = {}
        qwaited = set()
        wt = self.waited.setdefault(sect, {})

        def need(who, val):
            if who == src and who.startswith("q"):
                return
            if who.startswith("q"):
                val = max(val, self.qcnt.get(who, 0))
                qwaited.add(who)
            if wt.get(who, -1) >= val or waits.get(who, -1) >= val:
                return
            waits[who] = val

        for k in reads:
            lw = self.last_w.get(k)
            if lw:
                need(*lw)
        for k in writes:
            lw = self.last_w.get(k)
            if lw and not (lw[0] == sect == "T"):
                need(*lw)
            for who, val in self.readers.get(k, {}).items():
                need(who, val)
        if queue is not None:
            for who, val in list(self.readers.get("@" + src, {}).items()):
                need(who, val)
            self.readers["@" + src] = {}
        for who, val in waits.items():
            wt[who] = val

        if queue is None:
            self.tick[src] = self.tick.get(src, 0) + 1
            me = (src, self.tick[src])
        else:
            self.qcnt[src] = self.qcnt.get(src, 0) + 16
            me = (src, self.qcnt[src])
        for q in qwaited:
            self.readers.setdefault("@" + q, {})[me[0]] = me[1]
        for k in reads:
            self.readers.setdefault(k, {})[src] = me[1]
        for k in writes:
            self.last_w[k] = me
            self.readers[k] = {}
        self.pending.setdefault(sect, []).append(
            (sorted(waits.items()), meth, args, kwargs, me))
        return me

    def replay(self, sect, eng, semmap):
        for waits, meth, args, kwargs, me in self.pending.get(sect, []):
            for who, val in waits:
                eng.wait_ge(semmap[who], val)
            ins = getattr(eng, meth)(*args, **kwargs)
            ins.then_inc(semmap[me[0]], 16 if me[0].startswith("q") else 1)

    def final_waits(self, sect, eng, semmap):
        for who in sorted(set(list(self.tick) + list(self.qcnt))):
            val = self.qcnt.get(who, self.tick.get(who))
            eng.wait_ge(semmap[who], val)


def build_program(B, T, stages=4):
    TJ = T // 128
    NT = B * TJ
    NL = NT * NEXP
    nc = bass.Bass("TRN2", target_bir_lowering=False)

    def din(name, shape, dtype=F32):
        return nc.dram_tensor(name, list(shape), dtype, kind="ExternalInput").ap()

    x_in = din("x", (B, T, D))
    qk_w = din("qk_w", (E, D))
    v_w = din("v_w", (E, D))
    g_w = din("g_w", (E, D))
    o_w = din("o_w", (D, E))
    w_expert = din("w_expert", (D, D))
    w_gate_r = din("w_gate_r", (128, 8, NEXP))
    convw_r = din("convw_r", (128, 4, 8))
    convb_r = din("convb_r", (128, 8))
    gb4 = din("gb4", (4, 512))
    bx2 = din("bx2", (2, 512))
    fq4 = din("fq4", (4, 512))
    lng2 = din("lng2", (2, 512))
    lnb2 = din("lnb2", (2, 512))
    gam16 = din("gam16", (1, H))
    I128 = din("I128", (128, 128), BF16)
    Ltri = din("Ltri", (128, 128), BF16)
    Ssh_in = din("Ssh", (128, 128))
    iota8_in = din("iota8", (128, NEXP))
    tcols_in = din("tcols", (128, TJ))
    ones128bf = din("ones128bf", (128, 1), BF16)
    onesf32_in = din("onesf32", (1, 128))
    sel8_in = din("sel8", (128, NEXP), BF16)
    prevmask = din("prevmask", (NCORES, 1))

    out = nc.dram_tensor("out", [B, T, D], F32, kind="ExternalOutput").ap()
    x1s = nc.dram_tensor("x1s", [B, T, D], F32, kind="Internal").ap()
    fTs = nc.dram_tensor("fTs", [NT, 128, D], BF16, kind="Internal").ap()
    cc_in = nc.dram_tensor("cc_in", [1, NEXP], F32, kind="Internal").ap()
    cc_out = nc.dram_tensor("cc_out", [NCORES, NEXP], F32, kind="Internal",
                            addr_space="Shared").ap()

    def sal(name, shape, dtype=F32):
        return nc.alloc_sbuf_tensor(name, list(shape), dtype).ap()

    wqkT = sal("wqkT", [128, 8, E], BF16)
    wvT = sal("wvT", [128, 8, E], BF16)
    wgT = sal("wgT", [128, 8, E], BF16)
    woT = sal("woT", [128, 16, D], BF16)
    wgate = sal("wgate", [128, 8, NEXP], BF16)
    x0b = [sal(f"x0_{i}", [128, D]) for i in range(3)]
    abf2 = [sal(f"abf_{i}", [128, D], BF16) for i in range(2)]
    stat2 = [sal(f"stat_{i}", [128, 64]) for i in range(2)]
    diagw = sal("diagw", [128, 32, 128], BF16)
    cosb = sal("cosb", [128, E], BF16)
    decay = sal("decay", [128, TJ * H])
    tcols = sal("tcols_s", [128, TJ])
    iota8 = sal("iota8_s", [128, NEXP])
    sel8 = sal("sel8_s", [128, NEXP], BF16)
    o128b = sal("o128b", [128, 1], BF16)
    prevm = sal("prevm", [NCORES, 1])
    It = sal("It", [128, 128], BF16)
    Ssh = sal("Ssh_s", [128, 128])
    Lt = sal("Lt", [128, 128], BF16)
    aTb = [sal(f"aT_{i}", [128, 8, 131], BF16) for i in range(2)]
    aTtl = sal("aTtl", [128, B, 8, 3], BF16)
    xcT = sal("xcT", [128, 8, 128], BF16)
    q_t = sal("q_t", [128, E])
    tA = sal("tA", [128, E])
    ar1 = sal("ar1", [128, E], BF16)
    x1t2 = [sal(f"x1t{i}", [128, D]) for i in range(2)]
    cst = sal("cst", [128, 6])

    lngr = sal("lngr", [128, D], BF16)
    lnbr = sal("lnbr", [128, D], BF16)
    logit = sal("logit", [128, NL], mybir.dt.float16)
    gateA = sal("gateA", [128, NT])
    bestA = sal("bestA", [128, NT])
    strips = sal("strips", [128, 128])
    pool = sal("pool", [128, 2048])
    cwc = sal("cwc", [128, 4, 8])
    cbc = sal("cbc", [128, 8])
    stats = sal("stats", [128, 64])
    allc = sal("allc", [NCORES, NEXP])
    offs = sal("offs", [1, 16])
    mk8 = sal("mk8", [128, NEXP])
    mk8b = sal("mk8b", [128, NEXP], BF16)
    km8 = sal("km8", [128, NEXP])
    csP = [(x1t2[0][:, 0:256], x1t2[0][:, 256:512]),
           (x1t2[0][:, 512:768], x1t2[0][:, 768:1024])]

    pA = nc.alloc_psum_tensor("pA", [128, 2048], F32).ap()
    pB = nc.alloc_psum_tensor("pB", [128, 2048], F32).ap()
    pAk = ["pA0", "pA1", "pA2", "pA3"]
    pBk = ["pB0", "pB1", "pB2", "pB3"]

    wex = wqkT

    em = Emit()
    E_ = em.op

    def ln_statsA(src_ap, junk_ap, ks, kj, stt, kst):
        E_("S", "activation", junk_ap, src_ap, AF.Square, accum_out=stt[:, 0:1],
           reads=list(ks), writes=[kj, kst])
        E_("S", "activation", junk_ap, src_ap, AF.Identity, accum_out=stt[:, 1:2],
           reads=list(ks), writes=[kj, kst])

    def ln_statsB(width, stt, kst):
        S_sx, S_mean, S_var = stt[:, 1:2], stt[:, 2:3], stt[:, 3:4]
        S_rstd, S_nb, S_t1 = stt[:, 4:5], stt[:, 5:6], stt[:, 6:7]
        E_("V", "tensor_scalar", S_mean, S_sx, 1.0 / width, None, op0=ALU.mult,
           reads=[kst], writes=[kst])
        E_("V", "scalar_tensor_tensor", S_var, S_sx, S_mean, stt[:, 0:1],
           op0=ALU.mult, op1=ALU.subtract, reads=[kst], writes=[kst])
        E_("V", "tensor_scalar", S_var, S_var, -1.0 / width, EPS_LN,
           op0=ALU.mult, op1=ALU.add, reads=[kst], writes=[kst])
        E_("V", "reciprocal", S_t1, S_var, reads=[kst], writes=[kst])
        E_("S", "activation", S_rstd, S_t1, AF.Sqrt, reads=[kst], writes=[kst])
        E_("V", "tensor_scalar", S_nb, S_mean, S_rstd, -1.0,
           op0=ALU.mult, op1=ALU.mult, reads=[kst], writes=[kst])
        return S_rstd, S_nb

    def ln_stats(src_ap, junk_ap, ks, kj, width, stt=None, kst=None):
        stt = stats if stt is None else stt
        kst = "st" if kst is None else kst
        ln_statsA(src_ap, junk_ap, ks, kj, stt, kst)
        return ln_statsB(width, stt, kst)

    def bc16(ap2d):
        return ap2d.unsqueeze(-1).broadcast_to((128, 16, HD))

    YQ = [f"qY{i}" for i in range(4)]
    GQC = {"ld": ["qG0", "qG1"], "acc": ["qG2", "qG3"], "st": ["qG4", "qG5"],
           "bulk": ["qG0", "qG1", "qG4", "qG5"]}
    rot = {"Y": [0], "ld": [0], "acc": [0], "st": [0], "bulk": [0]}

    def dmaY(*args, **kw):
        q = YQ[rot["Y"][0] % len(YQ)]
        rot["Y"][0] += 1
        return E_("Y", "dma_start", *args, queue=q, **kw)

    def dmaG(*args, qcls="ld", **kw):
        qs = GQC[qcls]
        q = qs[rot[qcls][0] % len(qs)]
        rot[qcls][0] += 1
        return E_("G", "dma_start", *args, queue=q, **kw)

    dmaY(out=It[:], in_=I128[:, :], writes=["It"])
    dmaY(out=Lt[:], in_=Ltri[:, :], writes=["Lt"])
    dmaY(out=Ssh[:], in_=Ssh_in[:, :], writes=["Ssh"])
    dmaY(out=iota8[:], in_=iota8_in[:, :], writes=["io8"])
    dmaY(out=tcols[:], in_=tcols_in[:, :], writes=["tc"])
    dmaY(out=o128b[:], in_=ones128bf[:, :], writes=["o128"])
    dmaY(out=sel8[:], in_=sel8_in[:, :], writes=["sel8"])
    dmaY(out=prevm[:], in_=prevmask[:, :], writes=["prevm"])
    dmaY(out=cwc[:], in_=convw_r[:, :, :], writes=["cwc"])
    dmaY(out=cbc[:], in_=convb_r[:, :], writes=["cbc"])
    dmaG(out=wgate[:], in_=w_gate_r[:, :, :], writes=["wgate"])
    GBP = [(0, 512), (32, 512), (64, 512), (64, 1024)]
    BXP = [(0, 1536), (0, 512)]
    BXOVL = ["stg", "gb0"]
    FQP = [(0, 1024), (32, 1024), (32, 1536), (64, 1536)]
    STG = (0, 1536)
    for r in range(4):
        dmaY(out=pool[GBP[r][0]:GBP[r][0] + 1, GBP[r][1]:GBP[r][1] + 512],
             in_=gb4[r:r + 1, :], writes=[f"gb{r}"])
    for r in range(3):
        dmaY(out=pool[32 * r:32 * r + 1, 0:128], in_=onesf32_in[:, :],
             writes=["ones"])
    dmaY(out=pool[0:1, 128:128 + H], in_=gam16[:, :], writes=["gam"])
    for r in range(4):
        dmaY(out=pool[FQP[r][0]:FQP[r][0] + 1, FQP[r][1]:FQP[r][1] + 512],
             in_=fq4[r:r + 1, :], writes=[f"fqs{r}"])

    onesr = pool[0:1, 0:128]

    stg = pool[STG[0]:STG[0] + 1, STG[1]:STG[1] + 512]
    for c, v in ((0, math.pi / 2), (1, 0.25), (2, INV_2PI),
                 (3, RND_C), (4, -RND_C)):
        E_("V", "memset", cst[:, c:c + 1], v, writes=["cst"])
    C_PIH, C_QTR, C_I2P, C_RND, C_MRND = (cst[:, c:c + 1] for c in range(5))
    for ch in range(8):
        for k in range(4):
            E_("V", "tensor_scalar", diagw[:, 4 * ch + k, :], It[:],
               cwc[:, k, ch:ch + 1], None, op0=ALU.mult,
               reads=["It", "cwc"], writes=["diagw"])
    for srcd, dstt, key in ((lng2, lngr, "lngr"), (lnb2, lnbr, "lnbr")):
        for r in range(2):
            dmaY(out=stg, in_=srcd[r:r + 1, :], writes=["stg"])
            E_("T", "matmul", pB[:, 512 * r:512 * (r + 1)], onesr, stg,
               start=True, stop=True, reads=["ones", "stg"], writes=[pBk[r]])
        E_("V", "tensor_copy", dstt[:], pB[:, 0:1024],
           reads=[pBk[0], pBk[1]], writes=[key])

    gam = pool[0:1, 128:128 + H]
    u16 = pool[0:1, 144:144 + H]
    lg16 = pool[0:1, 160:160 + H]
    t16 = pool[0:1, 176:176 + H]
    E_("V", "tensor_scalar", u16, gam, -1.0, None, op0=ALU.add,
       reads=["gam"], writes=["u16"])
    E_("V", "tensor_scalar", lg16, u16, 1.0 / 5.0, None, op0=ALU.mult,
       reads=["u16"], writes=["lg16"])
    for cf in (1.0 / 4.0, 1.0 / 3.0, 1.0 / 2.0, 1.0):
        E_("V", "tensor_scalar", t16, lg16, -1.0, cf, op0=ALU.mult, op1=ALU.add,
           reads=["lg16"], writes=["t16"])
        E_("V", "tensor_mul", lg16, t16, u16, reads=["t16", "u16"], writes=["lg16"])
    E_("T", "matmul", pB[:, 0:H], onesr, lg16, start=True, stop=True,
       reads=["ones", "lg16"], writes=[pBk[0]])
    E_("V", "tensor_copy", stats[:, 32:32 + H], pB[:, 0:H],
       reads=[pBk[0]], writes=["st"])
    for j in range(TJ):
        E_("S", "activation", decay[:, H * j:H * (j + 1)],
           stats[:, 32:32 + H], AF.Exp, scale=tcols[:, j:j + 1],
           reads=["st", "tc"], writes=["decay"])

    def XK(slot):
        return [f"x0.{slot}.{c}" for c in range(4)]

    arh = [ar1[:, 0:1024], ar1[:, 1024:2048]]
    wjobs = []
    for wsrc, dstt, key in ((qk_w, wqkT, "wqkT"), (g_w, wgT, "wgT"),
                            (v_w, wvT, "wvT")):
        for sl in range(16):
            wjobs.append((wsrc, 128 * sl, dstt, key, sl, 8))
    for sl in range(8):
        wjobs.append((o_w, 128 * sl, woT, "woT", sl, 16))
    for s, (wsrc, r0, dstt, key, sl, nch) in enumerate(wjobs):
        sb = s % 2
        psum, psk = (pA, pAk) if sb == 0 else (pB, pBk)
        last = s == len(wjobs) - 1
        if nch == 8:
            kst = f"arw{sb}"
            dmaG(out=x0b[sb][:], in_=wsrc[r0:r0 + 128, :], writes=XK(sb),
                 qcls="bulk")
            E_("S", "copy", out=arh[sb], in_=x0b[sb][:],
               reads=XK(sb), writes=[kst])
            stg_bf = arh[sb]
        else:
            kst = "arw0" if sb == 0 else "arw1"
            dmaG(out=tA[:], in_=wsrc[r0:r0 + 128, :], writes=["tA"],
                 qcls="bulk")
            E_("S", "copy", out=ar1[:], in_=tA[:], reads=["tA"],
               writes=["arw0", "arw1"])
            stg_bf = ar1
        trd = [kst, "It"] + (["ar1", "arw0", "arw1"] if last else [])
        for ch in range(nch):
            E_("T", "matmul", psum[:, 128 * ch:128 * (ch + 1)],
               stg_bf[:, 128 * (ch % (nch if nch == 16 else 8)):
                      128 * (ch % (nch if nch == 16 else 8) + 1)], It[:],
               start=True, stop=True,
               reads=trd, writes=[psk[ch // 4]])
        if nch == 8:
            E_("V", "tensor_copy",
               dstt[:, :, 128 * sl:128 * (sl + 1)],
               psum[:, 0:1024].rearrange("p (c e) -> p c e", c=8),
               reads=[psk[0], psk[1]], writes=[key])
        else:
            E_("V", "tensor_copy", dstt[:, :, 128 * sl:128 * (sl + 1)],
               psum[:].rearrange("p (c e) -> p c e", c=16),
               reads=psk, writes=[key])

    proc = [(j, b) for j in range(TJ) for b in range(B)]
    if stages < 2:
        proc = []
    NT_run = len(proc)

    def load_x0(i):
        if i >= NT_run:
            return
        j, b = proc[i]
        dmaG(out=x0b[i % 3][:], in_=x_in[b, 128 * j:128 * (j + 1), :],
             writes=XK(i % 3))

    def build_cos(j):
        for c in range(4):
            w = c % 2
            xb, xkey = (x1t2[0], "x1t0") if w == 0 else (x1t2[1], "x1t1")
            csA, csB = xb[:, 0:512], xb[:, 512:1024]
            kk = f"csx{w}"
            wk = [kk, xkey] if c < 2 else [kk]
            pslot = pB[:, 1024 + 512 * w:1536 + 512 * w]
            pk = f"pB3c{w}"
            fr, fc = FQP[c]
            E_("T", "matmul", pslot, pool[fr:fr + 1, 0:128],
               pool[fr:fr + 1, fc:fc + 512], start=True, stop=True,
               reads=["ones", f"fqs{c}"], writes=[pk, pBk[2 + w]])
            E_("S", "activation", csA[:], pslot,
               AF.Identity, scale=tcols[:, j:j + 1],
               reads=[pk, pBk[2 + w], "tc"], writes=wk)
            E_("S", "activation", csB[:], csA[:], AF.Identity,
               scale=C_I2P, bias=C_QTR, reads=[kk, "cst"], writes=[kk])
            E_("S", "activation", csB[:], csB[:], AF.Identity, bias=C_RND,
               reads=[kk, "cst"], writes=[kk])
            E_("S", "activation", csB[:], csB[:], AF.Identity, bias=C_MRND,
               reads=[kk, "cst"], writes=[kk])
            E_("V", "scalar_tensor_tensor", csA[:], csB[:], -CW_C1, csA[:],
               op0=ALU.mult, op1=ALU.add, reads=[kk], writes=[kk])
            E_("V", "scalar_tensor_tensor", csA[:], csB[:], -CW_C2, csA[:],
               op0=ALU.mult, op1=ALU.add, reads=[kk], writes=[kk])
            E_("V", "tensor_scalar", csA[:], csA[:], CLAMP_HI, CLAMP_LO,
               op0=ALU.min, op1=ALU.max, reads=[kk], writes=[kk])
            E_("S", "activation", cosb[:, 512 * c:512 * (c + 1)],
               csA[:], AF.Sin, bias=C_PIH,
               reads=[kk, "cst", xkey], writes=["cos"])

    def fr_ln1A(k):
        if k >= NT_run:
            return
        p = k % 2
        ln_statsA(x0b[k % 3][:], abf2[p][:], XK(k % 3), f"abf{p}",
                  stat2[p], f"stp{p}")

    def fr_ln1B(k):
        if k >= NT_run:
            return
        p = k % 2
        rstd, nb = ln_statsB(D, stat2[p], f"stp{p}")
        E_("S", "activation", abf2[p][:], x0b[k % 3][:], AF.Identity,
           scale=rstd, bias=nb, reads=XK(k % 3) + [f"stp{p}"],
           writes=[f"abf{p}"])

    def fr_transp(k):
        if k >= NT_run:
            return
        j, b = proc[k]
        p = k % 2
        aT = aTb[p]
        kaT = f"aT{p}"
        if j == 0:
            E_("V", "memset", aT[:, :, 0:3], 0.0, writes=[kaT])
        else:
            E_("S", "copy", out=aT[:, :, 0:3], in_=aTtl[:, b],
               reads=[f"tl{b}"], writes=[kaT])
        for ch in range(8):
            E_("T", "matmul", pA[:, 128 * ch:128 * (ch + 1)],
               abf2[p][:, 128 * ch:128 * (ch + 1)], It[:],
               start=True, stop=True, reads=[f"abf{p}", "It"],
               writes=[pAk[ch // 4]])
        E_("S", "copy", out=aT[:, :, 3:131],
           in_=pA[:, 0:1024].rearrange("p (c e) -> p c e", c=8),
           reads=[pAk[0], pAk[1]], writes=[kaT])
        E_("S", "copy", out=aTtl[:, b], in_=aT[:, :, 128:131],
           reads=[kaT], writes=[f"tl{b}"])
        for ch in range(8):
            for k4 in range(4):
                E_("T", "matmul", pA[:, 1024 + 128 * ch:1024 + 128 * (ch + 1)],
                   diagw[:, 4 * ch + k4, :], aT[:, ch, k4:k4 + 128],
                   start=(k4 == 0), stop=(k4 == 3),
                   reads=["diagw", kaT], writes=[pAk[2 + ch // 4]])
        for ch in range(8):
            E_("S", "activation", xcT[:, ch, :],
               pA[:, 1024 + 128 * ch:1024 + 128 * (ch + 1)], AF.Identity,
               bias=cbc[:, ch:ch + 1],
               reads=["cbc", pAk[2 + ch // 4]], writes=["xcT"])

    def fr_qkA(k):
        if k >= NT_run:
            return
        for ch in range(4):
            for n in range(4):
                E_("T", "matmul", pA[:, 512 * n:512 * (n + 1)],
                   xcT[:, ch, :], wqkT[:, ch, 512 * n:512 * (n + 1)],
                   start=(ch == 0), stop=False,
                   reads=["xcT", "wqkT"], writes=[pAk[n]])

    def fr_qkB(k):
        if k >= NT_run:
            return
        for ch in range(4, 8):
            for n in range(4):
                E_("T", "matmul", pA[:, 512 * n:512 * (n + 1)],
                   xcT[:, ch, :], wqkT[:, ch, 512 * n:512 * (n + 1)],
                   start=False, stop=(ch == 7),
                   reads=["xcT", "wqkT"], writes=[pAk[n]])

    def fr_qnormA(k):
        if k >= NT_run:
            return
        j, b = proc[k]
        p = k % 2
        stp = stat2[p]
        kst = f"stp{p}"
        ssA, rinv, sc16 = stp[:, 8:24], stp[:, 24:40], stp[:, 40:56]
        E_("S", "activation", q_t[:], pA[:], AF.Square, reads=pAk, writes=["q"])
        E_("V", "tensor_reduce", ssA.unsqueeze(-1),
           q_t.rearrange("p (h d) -> p h d", h=16), axis=AX.X, op=ALU.add,
           reads=["q"], writes=[kst])
        E_("V", "reciprocal", ssA, ssA, reads=[kst], writes=[kst])
        E_("S", "activation", rinv, ssA, AF.Sqrt, reads=[kst], writes=[kst])
        E_("V", "tensor_scalar_min", rinv, rinv, 1e12, reads=[kst], writes=[kst])
        jH = H * j
        E_("V", "tensor_scalar", sc16, decay[:, jH:jH + H], 1e-6, None,
           op0=ALU.add, reads=["decay", kst], writes=[kst])
        E_("V", "reciprocal", sc16, sc16, reads=[kst], writes=[kst])

    def fr_qnormB(k):
        if k >= NT_run:
            return
        j, b = proc[k]
        p = k % 2
        rinv = stat2[p][:, 24:40]
        E_("V", "tensor_tensor", q_t.rearrange("p (h d) -> p h d", h=16),
           pA.rearrange("p (h d) -> p h d", h=16), bc16(rinv),
           op=ALU.mult, reads=pAk + [f"stp{p}"], writes=["q"])
        dmaY(out=strips[16 * b:16 * (b + 1), :], in_=q_t[127:128, :],
             reads=["q"], writes=[f"kst{b}"])

    def _gpair(npair):
        for ch in range(8):
            for n in npair:
                E_("T", "matmul", pB[:, 512 * n:512 * (n + 1)],
                   xcT[:, ch, :], wgT[:, ch, 512 * n:512 * (n + 1)],
                   start=(ch == 0), stop=False,
                   reads=["xcT", "wgT"], writes=[pBk[n]])
        for n in npair:
            r0, c0 = GBP[n]
            E_("T", "matmul", pB[:, 512 * n:512 * (n + 1)],
               pool[r0:r0 + 1, 0:128], pool[r0:r0 + 1, c0:c0 + 512],
               start=False, stop=True,
               reads=["ones", f"gb{n}"], writes=[pBk[n]])

    def fr_g1(k):
        if k >= NT_run:
            return
        _gpair((2, 3))

    def fr_g2(k):
        if k >= NT_run:
            return
        _gpair((0, 1))
        E_("S", "activation", ar1[:], pB[:], AF.Sigmoid, reads=pBk,
           writes=["ar1", "arh0", "arh1"])

    def bk_state1(k):
        j, b = proc[k]
        for n in range(4):
            E_("T", "matmul", pA[:, 512 * n:512 * (n + 1)], Ssh[:],
               q_t[:, 512 * n:512 * (n + 1)], start=True, stop=True,
               reads=["Ssh", "q"], writes=[pAk[n]])
        E_("V", "tensor_tensor", tA[:], cosb[:], pA[:], op=ALU.add,
           reads=["cos"] + pAk, writes=["tA"])
        if j > 0:
            dmaG(out=tA[0:1, :], in_=strips[16 * b:16 * (b + 1), :],
                 accum_op=ALU.add, reads=[f"kst{b}", "tA"], writes=["tA"],
                 qcls="acc")
        for ch in range(8):
            for n in range(4):
                E_("T", "matmul", pB[:, 512 * n:512 * (n + 1)],
                   xcT[:, ch, :], wvT[:, ch, 512 * n:512 * (n + 1)],
                   start=(ch == 0), stop=(ch == 7),
                   reads=["xcT", "wvT"], writes=[pBk[n]])

    def bk_state2(k):
        j, b = proc[k]
        p = k % 2
        jH = H * j
        E_("V", "tensor_mul", tA[:], tA[:], pB[:],
           reads=["tA"] + pBk, writes=["tA"])
        E_("V", "tensor_mul", tA[:], tA[:], ar1[:],
           reads=["tA", "ar1"], writes=["tA"])
        E_("V", "tensor_tensor", ar1.rearrange("p (h d) -> p h d", h=16),
           tA.rearrange("p (h d) -> p h d", h=16),
           bc16(decay[:, jH:jH + H]), op=ALU.mult,
           reads=["tA", "decay"], writes=["ar1"])
        E_("V", "tensor_tensor", q_t.rearrange("p (h d) -> p h d", h=16),
           q_t.rearrange("p (h d) -> p h d", h=16),
           bc16(stat2[p][:, 40:56]), op=ALU.mult,
           reads=["q", f"stp{p}"], writes=["q"])
        if j > 0:
            dmaG(out=ar1[0:1, :], in_=strips[16 * (4 + b):16 * (5 + b), :],
                 accum_op=ALU.add, reads=[f"cst{b}", "ar1"], writes=["ar1"],
                 qcls="acc")

    def bk_cum1(k):
        for n in range(4):
            E_("T", "matmul", pB[:, 512 * n:512 * (n + 1)], Lt[:],
               ar1[:, 512 * n:512 * (n + 1)], start=True, stop=True,
               reads=["Lt", "ar1"], writes=[pBk[n]])

    def bk_cum2(k):
        j, b = proc[k]
        E_("V", "tensor_mul", q_t[:], q_t[:], pB[:],
           reads=pBk + ["q"], writes=["q"])
        E_("S", "copy", out=ar1[:], in_=q_t[:], reads=["q"],
           writes=["ar1", "arh0", "arh1"])
        E_("S", "copy", out=tA[96:128, :], in_=pB[96:128, :],
           reads=pBk, writes=["tA"])
        dmaY(out=strips[16 * (4 + b):16 * (5 + b), :], in_=tA[127:128, :],
             reads=["tA"], writes=[f"cst{b}"])

    def bk_mqT(k):
        for h in range(2):
            for ch in range(8 * h, 8 * h + 8):
                E_("T", "matmul", pB[:, 128 * ch:128 * (ch + 1)],
                   ar1[:, 128 * ch:128 * (ch + 1)], It[:],
                   start=True, stop=True, reads=[f"arh{h}", "It"],
                   writes=[pBk[2 * h], pBk[2 * h + 1]])
            E_("S", "copy", out=ar1[:, 1024 * h:1024 * (h + 1)],
               in_=pB[:, 1024 * h:1024 * (h + 1)],
               reads=[pBk[2 * h], pBk[2 * h + 1]], writes=[f"arh{h}"])

    def bk_o(k):
        for ch in range(16):
            for n in range(2):
                E_("T", "matmul", pB[:, 512 * n:512 * (n + 1)],
                   ar1[:, 128 * ch:128 * (ch + 1)],
                   woT[:, ch, 512 * n:512 * (n + 1)],
                   start=(ch == 0), stop=(ch == 15),
                   reads=[f"arh{ch // 8}", "woT"], writes=[pBk[n]])

    def bk_lno(k):
        x1t, kx = x1t2[k % 2], f"x1t{k % 2}"
        E_("S", "activation", x1t[:], pB[:, 0:1024], AF.Identity,
           accum_out=stats[:, 1:2], reads=[pBk[0], pBk[1]],
           writes=[kx, "st"])
        E_("S", "activation", tA[:, 0:1024], x1t[:], AF.Square,
           accum_out=stats[:, 0:1], reads=[kx], writes=["tA", "st"])
        rstd1, nb1 = ln_statsB(D, stats, "st")
        E_("S", "activation", x1t[:], x1t[:], AF.Identity,
           scale=rstd1, bias=nb1, reads=[kx, "st"], writes=[kx])

    def bk_tailA1(k):
        x1t, kx = x1t2[k % 2], f"x1t{k % 2}"
        E_("V", "tensor_mul", x1t[:], x1t[:], lngr[:],
           reads=[kx, "lngr"], writes=[kx])
        E_("V", "tensor_add", x1t[:], x1t[:], lnbr[:],
           reads=[kx, "lnbr"], writes=[kx])
        ln_statsA(x1t[:], tA[:, 0:1024], [kx], "tA", stats, "st")

    def bk_tailA2(k):
        j, b = proc[k]
        xk = XK(k % 3)
        x1t, kx = x1t2[k % 2], f"x1t{k % 2}"
        rstd2, nb2 = ln_statsB(D, stats, "st")
        E_("V", "scalar_tensor_tensor", x1t[:], x1t[:], rstd2, x0b[k % 3][:],
           op0=ALU.mult, op1=ALU.add, reads=[kx, "st"] + xk, writes=[kx])
        E_("V", "tensor_scalar", x1t[:], x1t[:], nb2, None, op0=ALU.add,
           reads=[kx, "st"], writes=[kx])
        dmaG(out=x1s[b, 128 * j:128 * (j + 1), :], in_=x1t[:],
             reads=[kx], writes=[f"x1s.{b * TJ + j}"], qcls="st")

    def bk_tailB1(k):
        if k < 0:
            return
        p = k % 2
        x1t, kx = x1t2[p], f"x1t{p}"
        rstd3, nb3 = ln_stats(x1t[:], tA[:, 0:1024], [kx], "tA", D)
        E_("S", "activation", abf2[p][:], x1t[:], AF.Identity, scale=rstd3,
           bias=nb3, reads=[kx, "st"], writes=[f"abf{p}"])

    def bk_tailB2(k):
        if k < 0:
            return
        p = k % 2
        slot = proc[k][1] * TJ + proc[k][0]
        for ch in range(8):
            E_("T", "matmul", pA[:, 128 * ch:128 * (ch + 1)],
               abf2[p][:, 128 * ch:128 * (ch + 1)], It[:],
               start=True, stop=True, reads=[f"abf{p}", "It"],
               writes=[pAk[ch // 4]])
        E_("S", "copy", out=aTb[p][:, :, 3:131],
           in_=pA[:, 0:1024].rearrange("p (c e) -> p c e", c=8),
           reads=[pAk[0], pAk[1]], writes=[f"aT{p}"])
        for ch in range(8):
            E_("T", "matmul", pA[:, 1536:1536 + NEXP],
               aTb[p][:, ch, 3:131], wgate[:, ch, :],
               start=(ch == 0), stop=(ch == 7),
               reads=[f"aT{p}", "wgate"], writes=[pAk[3]])
        E_("V", "tensor_copy", logit[:, NEXP * slot:NEXP * (slot + 1)],
           pA[:, 1536:1536 + NEXP], reads=[pAk[3]], writes=["logit"])
        dmaG(out=fTs[slot, :, :], in_=aTb[p][:, :, 3:131],
             reads=[f"aT{p}"], writes=[f"fTs.{slot}"], qcls="st")

    if proc:
        load_x0(0)
        load_x0(1)
        load_x0(2)
        build_cos(0)
        fr_ln1A(0)
        fr_ln1B(0)
        fr_transp(0)
        fr_qkA(0)
        fr_qkB(0)
        fr_qnormA(0)
        fr_qnormB(0)
        fr_g1(0)
        fr_g2(0)
        fr_ln1A(1)

    for i in range(NT_run):
        j, b = proc[i]
        bk_state1(i)
        fr_ln1B(i + 1)
        bk_state2(i)
        fr_transp(i + 1)
        bk_tailB2(i - 2)
        fr_qkA(i + 1)
        bk_cum1(i)
        fr_qkB(i + 1)
        if i > 0:
            bk_tailA1(i - 1)
        bk_cum2(i)
        if i > 0:
            bk_tailA2(i - 1)
            load_x0(i + 2)
        bk_mqT(i)
        bk_o(i)
        fr_qnormA(i + 1)
        fr_qnormB(i + 1)
        bk_tailB1(i - 1)
        if b == B - 1 and j + 1 < TJ:
            build_cos(j + 1)
        fr_g1(i + 1)
        bk_lno(i)
        fr_g2(i + 1)
        fr_ln1A(i + 2)
    if NT_run:
        bk_tailA1(NT_run - 1)
        bk_tailA2(NT_run - 1)
        bk_tailB1(NT_run - 1)
        bk_tailB2(NT_run - 2)
        bk_tailB2(NT_run - 1)

    if stages < 2:
        dmaY(out=x0b[0][:], in_=x_in[0, 0:128, :], writes=XK(0))
        dmaY(out=out[0, 0:128, :], in_=x0b[0][:], reads=XK(0), writes=["o0"])
    if stages < 3:
        NT_p2 = 0
    else:
        NT_p2 = NT
    if stages >= 3:
        lgv = logit.rearrange("p (s e) -> p s e", s=NT)
        mxA = stats[:, 0:NT] if NT <= 64 else None
        exv = tA[:, 0:NL]
        exv3 = exv.rearrange("p (s e) -> p s e", s=NT)
        E_("V", "reduce_max", mxA.unsqueeze(-1), lgv, axis=AX.X,
           reads=["logit"], writes=["st"])
        E_("V", "tensor_tensor", exv3, lgv,
           mxA.unsqueeze(-1).broadcast_to((128, NT, NEXP)), op=ALU.subtract,
           reads=["logit", "st"], writes=["tA"])
        E_("S", "activation", exv, exv, AF.Exp, reads=["tA"], writes=["tA"])
        E_("V", "tensor_reduce", gateA.unsqueeze(-1), exv3, axis=AX.X, op=ALU.add,
           reads=["tA"], writes=["gateA"])
        E_("V", "reciprocal", gateA[:], gateA[:], reads=["gateA"], writes=["gateA"])
        E_("V", "tensor_tensor", exv3, lgv,
           mxA.unsqueeze(-1).broadcast_to((128, NT, NEXP)), op=ALU.is_equal,
           reads=["logit", "st"], writes=["tA"])
        iov = iota8.unsqueeze(1).broadcast_to((128, NT, NEXP))
        E_("V", "scalar_tensor_tensor", exv3, exv3, -999.0, iov,
           op0=ALU.mult, op1=ALU.add, reads=["tA", "io8"], writes=["tA"])
        E_("V", "tensor_reduce", bestA.unsqueeze(-1), exv3, axis=AX.X, op=ALU.min,
           reads=["tA"], writes=["bestA"])
        E_("V", "tensor_scalar", bestA[:], bestA[:], 999.0, None, op0=ALU.add,
           reads=["bestA"], writes=["bestA"])
        E_("V", "tensor_tensor", exv3, iov,
           bestA.unsqueeze(-1).broadcast_to((128, NT, NEXP)), op=ALU.is_equal,
           reads=["bestA", "io8"], writes=["tA"])
        mbv = ar1[:, 0:NL]
        E_("V", "tensor_copy", mbv, exv, reads=["tA"], writes=["ar1"])
        nsl = (NL + 127) // 128
        for c in range(nsl):
            w = min(128, NL - 128 * c)
            E_("T", "matmul", pA[0:w, c:c + 1], mbv[:, 128 * c:128 * c + w],
               o128b[:], start=True, stop=True,
               reads=["ar1", "o128"], writes=[pAk[0]])
        E_("V", "tensor_copy", stats[:, 0:nsl], pA[:, 0:nsl],
           reads=[pAk[0]], writes=["st"])
        E_("V", "tensor_copy", mk8b[:, 0:nsl], stats[:, 0:nsl],
           reads=["st"], writes=["mk8b"])
        for c in range(nsl):
            w = min(128, NL - 128 * c)
            E_("T", "matmul", pA[0:1, 8:16], mk8b[0:w, c:c + 1], sel8[0:w, :],
               start=(c == 0), stop=(c == nsl - 1),
               reads=["mk8b", "sel8"], writes=[pAk[0]])
        E_("V", "tensor_copy", allc[0:1, 0:NEXP], pA[0:1, 8:16],
           reads=[pAk[0]], writes=["allc"])
        E_("V", "tensor_copy", offs[0:1, 8:16], allc[0:1, 0:NEXP],
           reads=["allc"], writes=["mytot"])
        dmaG(out=cc_in[:, :], in_=offs[0:1, 8:16], reads=["mytot"], writes=["ccin"])
        E_("G", "collective_compute", "AllGather", ALU.bypass,
           replica_groups=[list(range(NCORES))], ins=[cc_in[:, :]],
           outs=[cc_out[:, :]], reads=["ccin"], writes=["ccout"])
        dmaG(out=allc[:], in_=cc_out[:, :], reads=["ccout"], writes=["allc"])
        E_("T", "matmul", pA[0:1, 24:32], prevm[:], allc[:],
           start=True, stop=True, reads=["allc", "prevm"], writes=[pAk[0]])
        E_("V", "tensor_copy", offs[0:1, 0:NEXP], pA[0:1, 24:32],
           reads=[pAk[0]], writes=["offs"])

        for r in range(2):
            dmaY(out=pool[BXP[r][0]:BXP[r][0] + 1, BXP[r][1]:BXP[r][1] + 512],
                 in_=bx2[r:r + 1, :], writes=[f"bx{r}", BXOVL[r]])
        for sl in range(8):
            dmaY(out=tA[:, 0:1024], in_=w_expert[128 * sl:128 * (sl + 1), :],
                 writes=["tA"])
            E_("S", "copy", out=wex[:, sl, 0:1024], in_=tA[:, 0:1024],
               reads=["tA", "tAh0", "tAh1"], writes=["wqkT"])

    def load_x1(s):
        if s >= NT_p2:
            return
        b, j = divmod(s, TJ)
        dmaG(out=x0b[s % 3][:], in_=x1s[b, 128 * j:128 * (j + 1), :],
             reads=[f"x1s.{s}"], writes=XK(s % 3))

    def p2_front(s):
        if s >= NT_p2:
            return
        p = s % 2
        dmaG(out=aTb[p][:, :, 3:131], in_=fTs[s, :, :],
             reads=[f"fTs.{s}"], writes=[f"aT{p}"])

    def p2_exp(s):
        p = s % 2
        off = 1024 * p
        pks = [pAk[2 * p], pAk[2 * p + 1]]
        for n in range(2):
            r0, c0 = BXP[n]
            E_("T", "matmul", pA[:, off + 512 * n:off + 512 * (n + 1)],
               pool[r0:r0 + 1, 0:128], pool[r0:r0 + 1, c0:c0 + 512],
               start=True, stop=False,
               reads=["ones", f"bx{n}"], writes=[pks[n]])
        for ch in range(8):
            for n in range(2):
                E_("T", "matmul", pA[:, off + 512 * n:off + 512 * (n + 1)],
                   aTb[p][:, ch, 3:131],
                   wex[:, ch, 512 * n:512 * (n + 1)],
                   start=False, stop=(ch == 7),
                   reads=[f"aT{p}", "wqkT"], writes=[pks[n]])
        E_("V", "tensor_scalar", mk8[:], iota8[:], bestA[:, s:s + 1], None,
           op0=ALU.is_equal, reads=["io8", "bestA"], writes=["mk8"])
        E_("V", "tensor_copy", mk8b[:], mk8[:], reads=["mk8"], writes=["mk8b"])
        E_("T", "matmul", pB[:, 1024:1024 + NEXP], onesr, offs[0:1, 0:NEXP],
           start=True, stop=False, reads=["ones", "offs"], writes=[pBk[2]])
        E_("T", "matmul", pB[:, 1024:1024 + NEXP], Lt[:], mk8b[:],
           start=False, stop=True, reads=["Lt", "mk8b"], writes=[pBk[2]])
        E_("T", "matmul", pB[0:1, 1536:1536 + NEXP], o128b[:], mk8b[:],
           start=True, stop=True, reads=["o128", "mk8b"], writes=[pBk[3]])
        E_("V", "tensor_add", offs[0:1, 0:NEXP], offs[0:1, 0:NEXP],
           pB[0:1, 1536:1536 + NEXP], reads=[pBk[3], "offs"], writes=["offs"])
        E_("V", "tensor_scalar", km8[:], pB[:, 1024:1024 + NEXP],
           CAP + 0.5, None, op0=ALU.is_lt, reads=[pBk[2]], writes=["km8"])
        E_("V", "tensor_mul", km8[:], km8[:], mk8[:],
           reads=["km8", "mk8"], writes=["km8"])
        keep = stats[:, 8:9]
        coef = stats[:, 9:10]
        E_("V", "tensor_reduce", keep.unsqueeze(-1), km8.unsqueeze(1),
           axis=AX.X, op=ALU.add, reads=["km8"], writes=["st"])
        E_("V", "tensor_mul", coef, keep, gateA[:, s:s + 1],
           reads=["st", "gateA"], writes=["st"])
        E_("S", "activation", tA[:, off:off + 1024], pA[:, off:off + 1024],
           AF.Identity, scale=coef, accum_out=stats[:, 1:2],
           reads=pks + ["st"], writes=[f"tAh{p}", "st"])

    def p2_fin(s):
        p = s % 2
        b, j = divmod(s, TJ)
        xk = XK(s % 3)
        x1a = x0b[s % 3]
        tAh = tA[:, 1024 * p:1024 * (p + 1)]
        E_("S", "activation", x1t2[1][:], tAh, AF.Square,
           accum_out=stats[:, 0:1], reads=[f"tAh{p}"], writes=["x1t1", "st"])
        rstd4, nb4 = ln_statsB(D, stats, "st")
        E_("V", "scalar_tensor_tensor", x1a[:], tAh, rstd4, x1a[:],
           op0=ALU.mult, op1=ALU.add, reads=[f"tAh{p}", "st"] + xk, writes=xk)
        E_("V", "tensor_scalar", x1a[:], x1a[:], nb4, None, op0=ALU.add,
           reads=xk + ["st"], writes=xk)
        dmaG(out=out[b, 128 * j:128 * (j + 1), :], in_=x1a[:],
             reads=xk, writes=[f"out.{s}"], qcls="st")

    if NT_p2 > 0:
        load_x1(0)
        load_x1(1)
        load_x1(2)
        p2_front(0)
    for s in range(NT_p2):
        p2_exp(s)
        p2_front(s + 1)
        p2_fin(s)
        load_x1(s + 3)

    import contextlib
    es = contextlib.ExitStack()
    semmap = {}
    for sname in sorted(set(list(em.tick) + list(em.qcnt))):
        semmap[sname] = es.enter_context(nc.semaphore(f"sm_{sname}"))
    block = es.enter_context(nc.Block())

    @block.sync
    def _(y):
        em.replay("Y", y, semmap)
        em.final_waits("Y", y, semmap)

    @block.tensor
    def _(t):
        em.replay("T", t, semmap)

    @block.vector
    def _(v):
        em.replay("V", v, semmap)

    @block.scalar
    def _(s):
        em.replay("S", s, semmap)

    @block.gpsimd
    def _(g):
        em.replay("G", g, semmap)

    es.close()
    return nc


def make_inputs(core_id, B, full):
    f32 = np.float32
    bfl = ml_dtypes.bfloat16
    T = full["L0_attn_x"].shape[1]
    TJ = T // 128
    x = np.ascontiguousarray(
        full["L0_attn_x"][core_id * B:(core_id + 1) * B]).astype(f32)
    convw = np.asarray(full["L0_attn_conv_w"]).reshape(D, 4)
    convw_r = np.ascontiguousarray(
        convw.reshape(8, 128, 4).transpose(1, 2, 0)).astype(f32)
    convb_r = np.ascontiguousarray(
        np.asarray(full["L0_attn_conv_b"]).reshape(8, 128).T).astype(f32)
    wgr = np.ascontiguousarray(np.asarray(full["L0_ffn_w_gate"]).reshape(
        8, 128, NEXP).transpose(1, 0, 2)).astype(f32)
    tc = np.arange(128, dtype=f32)[:, None] + 128.0 * np.arange(TJ, dtype=f32)[None, :]
    sel = np.zeros((128, NEXP), f32)
    sel[np.arange(128), np.arange(128) % NEXP] = 1.0
    pm = np.zeros((NCORES, 1), f32)
    pm[:core_id] = 1.0
    return {
        "x": x,
        "qk_w": np.asarray(full["L0_attn_qk_w"], dtype=f32),
        "v_w": np.asarray(full["L0_attn_v_w"], dtype=f32),
        "g_w": np.asarray(full["L0_attn_g_w"], dtype=f32),
        "o_w": np.asarray(full["L0_attn_o_w"], dtype=f32),
        "w_expert": np.asarray(full["L0_ffn_w_expert"], dtype=f32),
        "w_gate_r": wgr,
        "convw_r": convw_r,
        "convb_r": convb_r,
        "gb4": np.asarray(full["L0_attn_g_b"], dtype=f32).reshape(4, 512),
        "bx2": np.asarray(full["L0_ffn_b_expert"], dtype=f32).reshape(2, 512),
        "fq4": np.asarray(full["L0_attn_freqs"], dtype=f32).reshape(4, 512),
        "lng2": np.asarray(full["L0_attn_ln_g"], dtype=f32).reshape(2, 512),
        "lnb2": np.asarray(full["L0_attn_ln_b"], dtype=f32).reshape(2, 512),
        "gam16": np.asarray(full["L0_attn_gamma"], dtype=f32).reshape(1, H),
        "prevmask": pm,
        "I128": np.eye(128, dtype=bfl),
        "Ltri": np.triu(np.ones((128, 128), f32)).astype(bfl),
        "Ssh": np.eye(128, k=1, dtype=f32),
        "iota8": np.tile(np.arange(NEXP, dtype=f32), (128, 1)),
        "tcols": np.ascontiguousarray(tc),
        "ones128bf": np.ones((128, 1), dtype=bfl),
        "onesf32": np.ones((1, 128), dtype=f32),
        "sel8": np.ascontiguousarray(sel.astype(bfl)),
    }


_CACHE = {}


def kernel(**inputs):
    Bfull, T, _ = inputs["L0_attn_x"].shape
    B = Bfull // NCORES
    key = (B, T)
    if key not in _CACHE:
        _CACHE[key] = build_program(B, T)
    nc = _CACHE[key]
    in_maps = [make_inputs(c, B, inputs) for c in range(NCORES)]
    res = run_bass_kernel_spmd(nc, in_maps, core_ids=list(range(NCORES)))
    return np.concatenate([r["out"] for r in res.results], axis=0)


# version 32
# speedup vs baseline: 1.0326x; 1.0326x over previous
import math
import numpy as np
import ml_dtypes
import concourse.bass as bass
import concourse.mybir as mybir
from concourse.bass_utils import run_bass_kernel_spmd

F32 = mybir.dt.float32
BF16 = mybir.dt.bfloat16
I32 = mybir.dt.int32
AF = mybir.ActivationFunctionType
ALU = mybir.AluOpType
AX = mybir.AxisListType

NCORES = 8
D = 1024
E = 2048
H = 16
HD = 128
CAP = 100
NEXP = 8

TWO_PI = 2.0 * math.pi
CW_C1 = 6.28125
CW_C2 = float(np.float32(np.float32(TWO_PI) - np.float32(CW_C1)))
INV_2PI = float(np.float32(1.0 / TWO_PI))
RND_C = float(np.float32(1.5 * 2 ** 23))
CLAMP_HI = float(np.float32(math.pi / 2) - 3 * np.float32(2.0 ** -22))
CLAMP_LO = float(-np.float32(1.5 * math.pi) + 6 * np.float32(2.0 ** -22))
MAGIC = 0x5F3759DF
EPS_LN = 1e-5


class Emit:

    def __init__(self):
        self.tick = {}
        self.qcnt = {}
        self.last_w = {}
        self.readers = {}
        self.waited = {}
        self.pending = {}

    def op(self, sect, meth, *args, reads=(), writes=(), queue=None, **kwargs):
        src = queue if queue is not None else sect
        waits = {}
        qwaited = set()
        wt = self.waited.setdefault(sect, {})

        def need(who, val):
            if who == src and who.startswith("q"):
                return
            if who.startswith("q"):
                val = max(val, self.qcnt.get(who, 0))
                qwaited.add(who)
            if wt.get(who, -1) >= val or waits.get(who, -1) >= val:
                return
            waits[who] = val

        for k in reads:
            lw = self.last_w.get(k)
            if lw:
                need(*lw)
        for k in writes:
            lw = self.last_w.get(k)
            if lw and not (lw[0] == sect == "T"):
                need(*lw)
            for who, val in self.readers.get(k, {}).items():
                need(who, val)
        if queue is not None:
            for who, val in list(self.readers.get("@" + src, {}).items()):
                need(who, val)
            self.readers["@" + src] = {}
        for who, val in waits.items():
            wt[who] = val

        if queue is None:
            self.tick[src] = self.tick.get(src, 0) + 1
            me = (src, self.tick[src])
        else:
            self.qcnt[src] = self.qcnt.get(src, 0) + 16
            me = (src, self.qcnt[src])
        for q in qwaited:
            self.readers.setdefault("@" + q, {})[me[0]] = me[1]
        for k in reads:
            self.readers.setdefault(k, {})[src] = me[1]
        for k in writes:
            self.last_w[k] = me
            self.readers[k] = {}
        self.pending.setdefault(sect, []).append(
            (sorted(waits.items()), meth, args, kwargs, me))
        return me

    def replay(self, sect, eng, semmap):
        for waits, meth, args, kwargs, me in self.pending.get(sect, []):
            for who, val in waits:
                eng.wait_ge(semmap[who], val)
            ins = getattr(eng, meth)(*args, **kwargs)
            ins.then_inc(semmap[me[0]], 16 if me[0].startswith("q") else 1)

    def final_waits(self, sect, eng, semmap):
        for who in sorted(set(list(self.tick) + list(self.qcnt))):
            val = self.qcnt.get(who, self.tick.get(who))
            eng.wait_ge(semmap[who], val)


def build_program(B, T, stages=4):
    TJ = T // 128
    NT = B * TJ
    NL = NT * NEXP
    nc = bass.Bass("TRN2", target_bir_lowering=False)

    def din(name, shape, dtype=F32):
        return nc.dram_tensor(name, list(shape), dtype, kind="ExternalInput").ap()

    x_in = din("x", (B, T, D))
    qk_w = din("qk_w", (E, D))
    v_w = din("v_w", (E, D))
    g_w = din("g_w", (E, D))
    o_w = din("o_w", (D, E))
    w_expert = din("w_expert", (D, D))
    w_gate_r = din("w_gate_r", (128, 8, NEXP))
    convw_r = din("convw_r", (128, 4, 8))
    convb_r = din("convb_r", (128, 8))
    gb4 = din("gb4", (4, 512))
    bx2 = din("bx2", (2, 512))
    fq4 = din("fq4", (4, 512))
    lng2 = din("lng2", (2, 512))
    lnb2 = din("lnb2", (2, 512))
    gam16 = din("gam16", (1, H))
    I128 = din("I128", (128, 128), BF16)
    Ltri = din("Ltri", (128, 128), BF16)
    Ssh_in = din("Ssh", (128, 128))
    iota8_in = din("iota8", (128, NEXP))
    tcols_in = din("tcols", (128, TJ))
    ones128bf = din("ones128bf", (128, 1), BF16)
    onesf32_in = din("onesf32", (1, 128))
    sel8_in = din("sel8", (128, NEXP), BF16)
    prevmask = din("prevmask", (NCORES, 1))

    out = nc.dram_tensor("out", [B, T, D], F32, kind="ExternalOutput").ap()
    x1s = nc.dram_tensor("x1s", [B, T, D], F32, kind="Internal").ap()
    fTs = nc.dram_tensor("fTs", [NT, 128, D], BF16, kind="Internal").ap()
    cc_in = nc.dram_tensor("cc_in", [1, NEXP], F32, kind="Internal").ap()
    cc_out = nc.dram_tensor("cc_out", [NCORES, NEXP], F32, kind="Internal",
                            addr_space="Shared").ap()

    def sal(name, shape, dtype=F32):
        return nc.alloc_sbuf_tensor(name, list(shape), dtype).ap()

    wqkT = sal("wqkT", [128, 8, E], BF16)
    wvT = sal("wvT", [128, 8, E], BF16)
    wgT = sal("wgT", [128, 8, E], BF16)
    woT = sal("woT", [128, 16, D], BF16)
    wgate = sal("wgate", [128, 8, NEXP], BF16)
    x0b = [sal(f"x0_{i}", [128, D]) for i in range(3)]
    abf2 = [sal(f"abf_{i}", [128, D], BF16) for i in range(2)]
    stat2 = [sal(f"stat_{i}", [128, 64]) for i in range(2)]
    diagw = sal("diagw", [128, 32, 128], BF16)
    cosb = sal("cosb", [128, E], BF16)
    decay = sal("decay", [128, TJ * H])
    tcols = sal("tcols_s", [128, TJ])
    iota8 = sal("iota8_s", [128, NEXP])
    sel8 = sal("sel8_s", [128, NEXP], BF16)
    o128b = sal("o128b", [128, 1], BF16)
    prevm = sal("prevm", [NCORES, 1])
    It = sal("It", [128, 128], BF16)
    Ssh = sal("Ssh_s", [128, 128])
    Lt = sal("Lt", [128, 128], BF16)
    aTb = [sal(f"aT_{i}", [128, 8, 131], BF16) for i in range(2)]
    aTtl = sal("aTtl", [128, B, 8, 3], BF16)
    xcT = sal("xcT", [128, 8, 128], BF16)
    q_t = sal("q_t", [128, E])
    tA = sal("tA", [128, E])
    ar1 = sal("ar1", [128, E], BF16)
    x1t2 = [sal(f"x1t{i}", [128, D]) for i in range(2)]
    cst = sal("cst", [128, 6])

    lngr = sal("lngr", [128, D], BF16)
    lnbr = sal("lnbr", [128, D], BF16)
    logit = sal("logit", [128, NL], mybir.dt.float16)
    gateA = sal("gateA", [128, NT])
    bestA = sal("bestA", [128, NT])
    strips = sal("strips", [128, 128])
    pool = sal("pool", [128, 2048])
    cwc = sal("cwc", [128, 4, 8])
    cbc = sal("cbc", [128, 8])
    stats = sal("stats", [128, 64])
    allc = sal("allc", [NCORES, NEXP])
    offs = sal("offs", [1, 16])
    mk8 = sal("mk8", [128, NEXP])
    mk8b = sal("mk8b", [128, NEXP], BF16)
    km8 = sal("km8", [128, NEXP])
    csP = [(x1t2[0][:, 0:256], x1t2[0][:, 256:512]),
           (x1t2[0][:, 512:768], x1t2[0][:, 768:1024])]

    pA = nc.alloc_psum_tensor("pA", [128, 2048], F32).ap()
    pB = nc.alloc_psum_tensor("pB", [128, 2048], F32).ap()
    pAk = ["pA0", "pA1", "pA2", "pA3"]
    pBk = ["pB0", "pB1", "pB2", "pB3"]

    wex = wqkT

    em = Emit()
    E_ = em.op

    def ln_statsA(src_ap, junk_ap, ks, kj, stt, kst):
        E_("S", "activation", junk_ap, src_ap, AF.Square, accum_out=stt[:, 0:1],
           reads=list(ks), writes=[kj, kst])
        E_("S", "activation", junk_ap, src_ap, AF.Identity, accum_out=stt[:, 1:2],
           reads=list(ks), writes=[kj, kst])

    def ln_statsB(width, stt, kst):
        S_sx, S_mean, S_var = stt[:, 1:2], stt[:, 2:3], stt[:, 3:4]
        S_rstd, S_nb, S_t1 = stt[:, 4:5], stt[:, 5:6], stt[:, 6:7]
        E_("V", "tensor_scalar", S_mean, S_sx, 1.0 / width, None, op0=ALU.mult,
           reads=[kst], writes=[kst])
        E_("V", "scalar_tensor_tensor", S_var, S_sx, S_mean, stt[:, 0:1],
           op0=ALU.mult, op1=ALU.subtract, reads=[kst], writes=[kst])
        E_("V", "tensor_scalar", S_var, S_var, -1.0 / width, EPS_LN,
           op0=ALU.mult, op1=ALU.add, reads=[kst], writes=[kst])
        E_("V", "reciprocal", S_t1, S_var, reads=[kst], writes=[kst])
        E_("S", "activation", S_rstd, S_t1, AF.Sqrt, reads=[kst], writes=[kst])
        E_("V", "tensor_scalar", S_nb, S_mean, S_rstd, -1.0,
           op0=ALU.mult, op1=ALU.mult, reads=[kst], writes=[kst])
        return S_rstd, S_nb

    def ln_stats(src_ap, junk_ap, ks, kj, width, stt=None, kst=None):
        stt = stats if stt is None else stt
        kst = "st" if kst is None else kst
        ln_statsA(src_ap, junk_ap, ks, kj, stt, kst)
        return ln_statsB(width, stt, kst)

    def bc16(ap2d):
        return ap2d.unsqueeze(-1).broadcast_to((128, 16, HD))

    YQ = [f"qY{i}" for i in range(4)]
    GQC = {"ld": ["qG0", "qG1"], "acc": ["qG2", "qG3"], "st": ["qG4", "qG5"],
           "bulk": ["qG0", "qG1", "qG2", "qG3", "qG4", "qG5"]}
    rot = {"Y": [0], "ld": [0], "acc": [0], "st": [0], "bulk": [0]}

    def dmaY(*args, **kw):
        q = YQ[rot["Y"][0] % len(YQ)]
        rot["Y"][0] += 1
        return E_("Y", "dma_start", *args, queue=q, **kw)

    def dmaG(*args, qcls="ld", **kw):
        qs = GQC[qcls]
        q = qs[rot[qcls][0] % len(qs)]
        rot[qcls][0] += 1
        return E_("G", "dma_start", *args, queue=q, **kw)

    dmaY(out=It[:], in_=I128[:, :], writes=["It"])
    dmaY(out=Lt[:], in_=Ltri[:, :], writes=["Lt"])
    dmaY(out=Ssh[:], in_=Ssh_in[:, :], writes=["Ssh"])
    dmaY(out=iota8[:], in_=iota8_in[:, :], writes=["io8"])
    dmaY(out=tcols[:], in_=tcols_in[:, :], writes=["tc"])
    dmaY(out=o128b[:], in_=ones128bf[:, :], writes=["o128"])
    dmaY(out=sel8[:], in_=sel8_in[:, :], writes=["sel8"])
    dmaY(out=prevm[:], in_=prevmask[:, :], writes=["prevm"])
    dmaY(out=cwc[:], in_=convw_r[:, :, :], writes=["cwc"])
    dmaY(out=cbc[:], in_=convb_r[:, :], writes=["cbc"])
    dmaG(out=wgate[:], in_=w_gate_r[:, :, :], writes=["wgate"])
    GBP = [(0, 512), (32, 512), (64, 512), (64, 1024)]
    BXP = [(0, 1536), (0, 512)]
    BXOVL = ["stg", "gb0"]
    FQP = [(0, 1024), (32, 1024), (32, 1536), (64, 1536)]
    STG = (0, 1536)
    for r in range(4):
        dmaY(out=pool[GBP[r][0]:GBP[r][0] + 1, GBP[r][1]:GBP[r][1] + 512],
             in_=gb4[r:r + 1, :], writes=[f"gb{r}"])
    for r in range(3):
        dmaY(out=pool[32 * r:32 * r + 1, 0:128], in_=onesf32_in[:, :],
             writes=["ones"])
    dmaY(out=pool[0:1, 128:128 + H], in_=gam16[:, :], writes=["gam"])
    for r in range(4):
        dmaY(out=pool[FQP[r][0]:FQP[r][0] + 1, FQP[r][1]:FQP[r][1] + 512],
             in_=fq4[r:r + 1, :], writes=[f"fqs{r}"])

    onesr = pool[0:1, 0:128]

    stg = pool[STG[0]:STG[0] + 1, STG[1]:STG[1] + 512]
    for c, v in ((0, math.pi / 2), (1, 0.25), (2, INV_2PI),
                 (3, RND_C), (4, -RND_C)):
        E_("V", "memset", cst[:, c:c + 1], v, writes=["cst"])
    C_PIH, C_QTR, C_I2P, C_RND, C_MRND = (cst[:, c:c + 1] for c in range(5))
    for ch in range(8):
        for k in range(4):
            E_("V", "tensor_scalar", diagw[:, 4 * ch + k, :], It[:],
               cwc[:, k, ch:ch + 1], None, op0=ALU.mult,
               reads=["It", "cwc"], writes=["diagw"])
    for srcd, dstt, key in ((lng2, lngr, "lngr"), (lnb2, lnbr, "lnbr")):
        for r in range(2):
            dmaY(out=stg, in_=srcd[r:r + 1, :], writes=["stg"])
            E_("T", "matmul", pB[:, 512 * r:512 * (r + 1)], onesr, stg,
               start=True, stop=True, reads=["ones", "stg"], writes=[pBk[r]])
        E_("V", "tensor_copy", dstt[:], pB[:, 0:1024],
           reads=[pBk[0], pBk[1]], writes=[key])

    gam = pool[0:1, 128:128 + H]
    u16 = pool[0:1, 144:144 + H]
    lg16 = pool[0:1, 160:160 + H]
    t16 = pool[0:1, 176:176 + H]
    E_("V", "tensor_scalar", u16, gam, -1.0, None, op0=ALU.add,
       reads=["gam"], writes=["u16"])
    E_("V", "tensor_scalar", lg16, u16, 1.0 / 5.0, None, op0=ALU.mult,
       reads=["u16"], writes=["lg16"])
    for cf in (1.0 / 4.0, 1.0 / 3.0, 1.0 / 2.0, 1.0):
        E_("V", "tensor_scalar", t16, lg16, -1.0, cf, op0=ALU.mult, op1=ALU.add,
           reads=["lg16"], writes=["t16"])
        E_("V", "tensor_mul", lg16, t16, u16, reads=["t16", "u16"], writes=["lg16"])
    E_("T", "matmul", pB[:, 0:H], onesr, lg16, start=True, stop=True,
       reads=["ones", "lg16"], writes=[pBk[0]])
    E_("V", "tensor_copy", stats[:, 32:32 + H], pB[:, 0:H],
       reads=[pBk[0]], writes=["st"])
    for j in range(TJ):
        E_("S", "activation", decay[:, H * j:H * (j + 1)],
           stats[:, 32:32 + H], AF.Exp, scale=tcols[:, j:j + 1],
           reads=["st", "tc"], writes=["decay"])

    def XK(slot):
        return [f"x0.{slot}.{c}" for c in range(4)]

    arh = [ar1[:, 0:1024], ar1[:, 1024:2048]]
    wjobs = []
    for wsrc, dstt, key in ((qk_w, wqkT, "wqkT"), (g_w, wgT, "wgT"),
                            (v_w, wvT, "wvT")):
        for sl in range(16):
            wjobs.append((wsrc, 128 * sl, dstt, key, sl, 8))
    for sl in range(8):
        wjobs.append((o_w, 128 * sl, woT, "woT", sl, 16))
    for s, (wsrc, r0, dstt, key, sl, nch) in enumerate(wjobs):
        sb = s % 2
        psum, psk = (pA, pAk) if sb == 0 else (pB, pBk)
        last = s == len(wjobs) - 1
        if nch == 8:
            kst = f"arw{sb}"
            dmaG(out=x0b[sb][:], in_=wsrc[r0:r0 + 128, :], writes=XK(sb),
                 qcls="bulk")
            E_("S", "copy", out=arh[sb], in_=x0b[sb][:],
               reads=XK(sb), writes=[kst])
            stg_bf = arh[sb]
        else:
            kst = "arw0" if sb == 0 else "arw1"
            dmaG(out=tA[:], in_=wsrc[r0:r0 + 128, :], writes=["tA"],
                 qcls="bulk")
            E_("S", "copy", out=ar1[:], in_=tA[:], reads=["tA"],
               writes=["arw0", "arw1"])
            stg_bf = ar1
        trd = [kst, "It"] + (["ar1", "arw0", "arw1"] if last else [])
        for ch in range(nch):
            E_("T", "matmul", psum[:, 128 * ch:128 * (ch + 1)],
               stg_bf[:, 128 * (ch % (nch if nch == 16 else 8)):
                      128 * (ch % (nch if nch == 16 else 8) + 1)], It[:],
               start=True, stop=True,
               reads=trd, writes=[psk[ch // 4]])
        if nch == 8:
            E_("V", "tensor_copy",
               dstt[:, :, 128 * sl:128 * (sl + 1)],
               psum[:, 0:1024].rearrange("p (c e) -> p c e", c=8),
               reads=[psk[0], psk[1]], writes=[key])
        else:
            E_("V", "tensor_copy", dstt[:, :, 128 * sl:128 * (sl + 1)],
               psum[:].rearrange("p (c e) -> p c e", c=16),
               reads=psk, writes=[key])

    proc = [(j, b) for j in range(TJ) for b in range(B)]
    if stages < 2:
        proc = []
    NT_run = len(proc)

    def load_x0(i):
        if i >= NT_run:
            return
        j, b = proc[i]
        dmaG(out=x0b[i % 3][:], in_=x_in[b, 128 * j:128 * (j + 1), :],
             writes=XK(i % 3))

    def build_cos(j):
        for c in range(4):
            w = c % 2
            xb, xkey = (x1t2[0], "x1t0") if w == 0 else (x1t2[1], "x1t1")
            csA, csB = xb[:, 0:512], xb[:, 512:1024]
            kk = f"csx{w}"
            wk = [kk, xkey] if c < 2 else [kk]
            pslot = pB[:, 1024 + 512 * w:1536 + 512 * w]
            pk = f"pB3c{w}"
            fr, fc = FQP[c]
            E_("T", "matmul", pslot, pool[fr:fr + 1, 0:128],
               pool[fr:fr + 1, fc:fc + 512], start=True, stop=True,
               reads=["ones", f"fqs{c}"], writes=[pk, pBk[2 + w]])
            E_("S", "activation", csA[:], pslot,
               AF.Identity, scale=tcols[:, j:j + 1],
               reads=[pk, pBk[2 + w], "tc"], writes=wk)
            E_("S", "activation", csB[:], csA[:], AF.Identity,
               scale=C_I2P, bias=C_QTR, reads=[kk, "cst"], writes=[kk])
            E_("S", "activation", csB[:], csB[:], AF.Identity, bias=C_RND,
               reads=[kk, "cst"], writes=[kk])
            E_("S", "activation", csB[:], csB[:], AF.Identity, bias=C_MRND,
               reads=[kk, "cst"], writes=[kk])
            E_("V", "scalar_tensor_tensor", csA[:], csB[:], -CW_C1, csA[:],
               op0=ALU.mult, op1=ALU.add, reads=[kk], writes=[kk])
            E_("V", "scalar_tensor_tensor", csA[:], csB[:], -CW_C2, csA[:],
               op0=ALU.mult, op1=ALU.add, reads=[kk], writes=[kk])
            E_("V", "tensor_scalar", csA[:], csA[:], CLAMP_HI, CLAMP_LO,
               op0=ALU.min, op1=ALU.max, reads=[kk], writes=[kk])
            E_("S", "activation", cosb[:, 512 * c:512 * (c + 1)],
               csA[:], AF.Sin, bias=C_PIH,
               reads=[kk, "cst", xkey], writes=["cos"])

    def fr_ln1A(k):
        if k >= NT_run:
            return
        p = k % 2
        ln_statsA(x0b[k % 3][:], abf2[p][:], XK(k % 3), f"abf{p}",
                  stat2[p], f"stp{p}")

    def fr_ln1B(k):
        if k >= NT_run:
            return
        p = k % 2
        rstd, nb = ln_statsB(D, stat2[p], f"stp{p}")
        E_("S", "activation", abf2[p][:], x0b[k % 3][:], AF.Identity,
           scale=rstd, bias=nb, reads=XK(k % 3) + [f"stp{p}"],
           writes=[f"abf{p}"])

    def fr_transp(k):
        if k >= NT_run:
            return
        j, b = proc[k]
        p = k % 2
        aT = aTb[p]
        kaT = f"aT{p}"
        if j == 0:
            E_("V", "memset", aT[:, :, 0:3], 0.0, writes=[kaT])
        else:
            E_("S", "copy", out=aT[:, :, 0:3], in_=aTtl[:, b],
               reads=[f"tl{b}"], writes=[kaT])
        for ch in range(8):
            E_("T", "matmul", pA[:, 128 * ch:128 * (ch + 1)],
               abf2[p][:, 128 * ch:128 * (ch + 1)], It[:],
               start=True, stop=True, reads=[f"abf{p}", "It"],
               writes=[pAk[ch // 4]])
        E_("S", "copy", out=aT[:, :, 3:131],
           in_=pA[:, 0:1024].rearrange("p (c e) -> p c e", c=8),
           reads=[pAk[0], pAk[1]], writes=[kaT])
        E_("S", "copy", out=aTtl[:, b], in_=aT[:, :, 128:131],
           reads=[kaT], writes=[f"tl{b}"])
        for ch in range(8):
            for k4 in range(4):
                E_("T", "matmul", pA[:, 1024 + 128 * ch:1024 + 128 * (ch + 1)],
                   diagw[:, 4 * ch + k4, :], aT[:, ch, k4:k4 + 128],
                   start=(k4 == 0), stop=(k4 == 3),
                   reads=["diagw", kaT], writes=[pAk[2 + ch // 4]])
        for ch in range(8):
            E_("S", "activation", xcT[:, ch, :],
               pA[:, 1024 + 128 * ch:1024 + 128 * (ch + 1)], AF.Identity,
               bias=cbc[:, ch:ch + 1],
               reads=["cbc", pAk[2 + ch // 4]], writes=["xcT"])

    def fr_qkA(k):
        if k >= NT_run:
            return
        for ch in range(4):
            for n in range(4):
                E_("T", "matmul", pA[:, 512 * n:512 * (n + 1)],
                   xcT[:, ch, :], wqkT[:, ch, 512 * n:512 * (n + 1)],
                   start=(ch == 0), stop=False,
                   reads=["xcT", "wqkT"], writes=[pAk[n]])

    def fr_qkB(k):
        if k >= NT_run:
            return
        for ch in range(4, 8):
            for n in range(4):
                E_("T", "matmul", pA[:, 512 * n:512 * (n + 1)],
                   xcT[:, ch, :], wqkT[:, ch, 512 * n:512 * (n + 1)],
                   start=False, stop=(ch == 7),
                   reads=["xcT", "wqkT"], writes=[pAk[n]])

    def fr_qnormA(k):
        if k >= NT_run:
            return
        j, b = proc[k]
        p = k % 2
        stp = stat2[p]
        kst = f"stp{p}"
        ssA, rinv, sc16 = stp[:, 8:24], stp[:, 24:40], stp[:, 40:56]
        E_("S", "activation", q_t[:], pA[:], AF.Square, reads=pAk, writes=["q"])
        E_("V", "tensor_reduce", ssA.unsqueeze(-1),
           q_t.rearrange("p (h d) -> p h d", h=16), axis=AX.X, op=ALU.add,
           reads=["q"], writes=[kst])
        E_("V", "reciprocal", ssA, ssA, reads=[kst], writes=[kst])
        E_("S", "activation", rinv, ssA, AF.Sqrt, reads=[kst], writes=[kst])
        E_("V", "tensor_scalar_min", rinv, rinv, 1e12, reads=[kst], writes=[kst])
        jH = H * j
        E_("V", "tensor_scalar", sc16, decay[:, jH:jH + H], 1e-6, None,
           op0=ALU.add, reads=["decay", kst], writes=[kst])
        E_("V", "reciprocal", sc16, sc16, reads=[kst], writes=[kst])

    def fr_qnormB(k):
        if k >= NT_run:
            return
        j, b = proc[k]
        p = k % 2
        rinv = stat2[p][:, 24:40]
        E_("V", "tensor_tensor", q_t.rearrange("p (h d) -> p h d", h=16),
           pA.rearrange("p (h d) -> p h d", h=16), bc16(rinv),
           op=ALU.mult, reads=pAk + [f"stp{p}"], writes=["q"])
        dmaY(out=strips[16 * b:16 * (b + 1), :], in_=q_t[127:128, :],
             reads=["q"], writes=[f"kst{b}"])

    def _gpair(npair):
        for ch in range(8):
            for n in npair:
                E_("T", "matmul", pB[:, 512 * n:512 * (n + 1)],
                   xcT[:, ch, :], wgT[:, ch, 512 * n:512 * (n + 1)],
                   start=(ch == 0), stop=False,
                   reads=["xcT", "wgT"], writes=[pBk[n]])
        for n in npair:
            r0, c0 = GBP[n]
            E_("T", "matmul", pB[:, 512 * n:512 * (n + 1)],
               pool[r0:r0 + 1, 0:128], pool[r0:r0 + 1, c0:c0 + 512],
               start=False, stop=True,
               reads=["ones", f"gb{n}"], writes=[pBk[n]])

    def fr_g1(k):
        if k >= NT_run:
            return
        _gpair((2, 3))

    def fr_g2(k):
        if k >= NT_run:
            return
        _gpair((0, 1))
        E_("S", "activation", ar1[:], pB[:], AF.Sigmoid, reads=pBk,
           writes=["ar1", "arh0", "arh1"])

    def bk_state1(k):
        j, b = proc[k]
        for n in range(4):
            E_("T", "matmul", pA[:, 512 * n:512 * (n + 1)], Ssh[:],
               q_t[:, 512 * n:512 * (n + 1)], start=True, stop=True,
               reads=["Ssh", "q"], writes=[pAk[n]])
        E_("V", "tensor_tensor", tA[:], cosb[:], pA[:], op=ALU.add,
           reads=["cos"] + pAk, writes=["tA"])
        if j > 0:
            dmaG(out=tA[0:1, :], in_=strips[16 * b:16 * (b + 1), :],
                 accum_op=ALU.add, reads=[f"kst{b}", "tA"], writes=["tA"],
                 qcls="acc")
        for ch in range(8):
            for n in range(4):
                E_("T", "matmul", pB[:, 512 * n:512 * (n + 1)],
                   xcT[:, ch, :], wvT[:, ch, 512 * n:512 * (n + 1)],
                   start=(ch == 0), stop=(ch == 7),
                   reads=["xcT", "wvT"], writes=[pBk[n]])

    def bk_state2(k):
        j, b = proc[k]
        p = k % 2
        jH = H * j
        E_("V", "tensor_mul", tA[:], tA[:], pB[:],
           reads=["tA"] + pBk, writes=["tA"])
        E_("V", "tensor_mul", tA[:], tA[:], ar1[:],
           reads=["tA", "ar1"], writes=["tA"])
        E_("V", "tensor_tensor", ar1.rearrange("p (h d) -> p h d", h=16),
           tA.rearrange("p (h d) -> p h d", h=16),
           bc16(decay[:, jH:jH + H]), op=ALU.mult,
           reads=["tA", "decay"], writes=["ar1"])
        E_("V", "tensor_tensor", q_t.rearrange("p (h d) -> p h d", h=16),
           q_t.rearrange("p (h d) -> p h d", h=16),
           bc16(stat2[p][:, 40:56]), op=ALU.mult,
           reads=["q", f"stp{p}"], writes=["q"])
        if j > 0:
            dmaG(out=ar1[0:1, :], in_=strips[16 * (4 + b):16 * (5 + b), :],
                 accum_op=ALU.add, reads=[f"cst{b}", "ar1"], writes=["ar1"],
                 qcls="acc")

    def bk_cum1(k):
        for n in range(4):
            E_("T", "matmul", pB[:, 512 * n:512 * (n + 1)], Lt[:],
               ar1[:, 512 * n:512 * (n + 1)], start=True, stop=True,
               reads=["Lt", "ar1"], writes=[pBk[n]])

    def bk_cum2(k):
        j, b = proc[k]
        E_("V", "tensor_mul", q_t[:], q_t[:], pB[:],
           reads=pBk + ["q"], writes=["q"])
        E_("S", "copy", out=ar1[:], in_=q_t[:], reads=["q"],
           writes=["ar1", "arh0", "arh1"])
        E_("S", "copy", out=tA[96:128, :], in_=pB[96:128, :],
           reads=pBk, writes=["tA"])
        dmaY(out=strips[16 * (4 + b):16 * (5 + b), :], in_=tA[127:128, :],
             reads=["tA"], writes=[f"cst{b}"])

    def bk_mqT(k):
        for h in range(2):
            for ch in range(8 * h, 8 * h + 8):
                E_("T", "matmul", pB[:, 128 * ch:128 * (ch + 1)],
                   ar1[:, 128 * ch:128 * (ch + 1)], It[:],
                   start=True, stop=True, reads=[f"arh{h}", "It"],
                   writes=[pBk[2 * h], pBk[2 * h + 1]])
            E_("S", "copy", out=ar1[:, 1024 * h:1024 * (h + 1)],
               in_=pB[:, 1024 * h:1024 * (h + 1)],
               reads=[pBk[2 * h], pBk[2 * h + 1]], writes=[f"arh{h}"])

    def bk_o(k):
        for ch in range(16):
            for n in range(2):
                E_("T", "matmul", pB[:, 512 * n:512 * (n + 1)],
                   ar1[:, 128 * ch:128 * (ch + 1)],
                   woT[:, ch, 512 * n:512 * (n + 1)],
                   start=(ch == 0), stop=(ch == 15),
                   reads=[f"arh{ch // 8}", "woT"], writes=[pBk[n]])

    def bk_lno(k):
        x1t, kx = x1t2[k % 2], f"x1t{k % 2}"
        E_("S", "activation", x1t[:], pB[:, 0:1024], AF.Identity,
           accum_out=stats[:, 1:2], reads=[pBk[0], pBk[1]],
           writes=[kx, "st"])
        E_("S", "activation", tA[:, 0:1024], x1t[:], AF.Square,
           accum_out=stats[:, 0:1], reads=[kx], writes=["tA", "st"])
        rstd1, nb1 = ln_statsB(D, stats, "st")
        E_("S", "activation", x1t[:], x1t[:], AF.Identity,
           scale=rstd1, bias=nb1, reads=[kx, "st"], writes=[kx])

    def bk_tailA1(k):
        x1t, kx = x1t2[k % 2], f"x1t{k % 2}"
        E_("V", "tensor_mul", x1t[:], x1t[:], lngr[:],
           reads=[kx, "lngr"], writes=[kx])
        E_("V", "tensor_add", x1t[:], x1t[:], lnbr[:],
           reads=[kx, "lnbr"], writes=[kx])
        ln_statsA(x1t[:], tA[:, 0:1024], [kx], "tA", stats, "st")

    def bk_tailA2(k):
        j, b = proc[k]
        xk = XK(k % 3)
        x1t, kx = x1t2[k % 2], f"x1t{k % 2}"
        rstd2, nb2 = ln_statsB(D, stats, "st")
        E_("V", "scalar_tensor_tensor", x1t[:], x1t[:], rstd2, x0b[k % 3][:],
           op0=ALU.mult, op1=ALU.add, reads=[kx, "st"] + xk, writes=[kx])
        E_("V", "tensor_scalar", x1t[:], x1t[:], nb2, None, op0=ALU.add,
           reads=[kx, "st"], writes=[kx])
        dmaG(out=x1s[b, 128 * j:128 * (j + 1), :], in_=x1t[:],
             reads=[kx], writes=[f"x1s.{b * TJ + j}"], qcls="st")

    def bk_tailB1(k):
        if k < 0:
            return
        p = k % 2
        x1t, kx = x1t2[p], f"x1t{p}"
        rstd3, nb3 = ln_stats(x1t[:], tA[:, 0:1024], [kx], "tA", D)
        E_("S", "activation", abf2[p][:], x1t[:], AF.Identity, scale=rstd3,
           bias=nb3, reads=[kx, "st"], writes=[f"abf{p}"])

    def bk_tailB2(k):
        if k < 0:
            return
        p = k % 2
        slot = proc[k][1] * TJ + proc[k][0]
        for ch in range(8):
            E_("T", "matmul", pA[:, 128 * ch:128 * (ch + 1)],
               abf2[p][:, 128 * ch:128 * (ch + 1)], It[:],
               start=True, stop=True, reads=[f"abf{p}", "It"],
               writes=[pAk[ch // 4]])
        E_("S", "copy", out=aTb[p][:, :, 3:131],
           in_=pA[:, 0:1024].rearrange("p (c e) -> p c e", c=8),
           reads=[pAk[0], pAk[1]], writes=[f"aT{p}"])
        for ch in range(8):
            E_("T", "matmul", pA[:, 1536:1536 + NEXP],
               aTb[p][:, ch, 3:131], wgate[:, ch, :],
               start=(ch == 0), stop=(ch == 7),
               reads=[f"aT{p}", "wgate"], writes=[pAk[3]])
        E_("V", "tensor_copy", logit[:, NEXP * slot:NEXP * (slot + 1)],
           pA[:, 1536:1536 + NEXP], reads=[pAk[3]], writes=["logit"])
        dmaG(out=fTs[slot, :, :], in_=aTb[p][:, :, 3:131],
             reads=[f"aT{p}"], writes=[f"fTs.{slot}"], qcls="st")

    if proc:
        load_x0(0)
        load_x0(1)
        load_x0(2)
        build_cos(0)
        fr_ln1A(0)
        fr_ln1B(0)
        fr_transp(0)
        fr_qkA(0)
        fr_qkB(0)
        fr_qnormA(0)
        fr_qnormB(0)
        fr_g1(0)
        fr_g2(0)
        fr_ln1A(1)

    for i in range(NT_run):
        j, b = proc[i]
        bk_state1(i)
        fr_ln1B(i + 1)
        bk_state2(i)
        fr_transp(i + 1)
        bk_tailB2(i - 2)
        fr_qkA(i + 1)
        bk_cum1(i)
        fr_qkB(i + 1)
        if i > 0:
            bk_tailA1(i - 1)
        bk_cum2(i)
        if i > 0:
            bk_tailA2(i - 1)
            load_x0(i + 2)
        bk_mqT(i)
        bk_o(i)
        fr_qnormA(i + 1)
        fr_qnormB(i + 1)
        bk_tailB1(i - 1)
        if b == B - 1 and j + 1 < TJ:
            build_cos(j + 1)
        fr_g1(i + 1)
        bk_lno(i)
        fr_g2(i + 1)
        fr_ln1A(i + 2)
    if NT_run:
        bk_tailA1(NT_run - 1)
        bk_tailA2(NT_run - 1)
        bk_tailB1(NT_run - 1)
        bk_tailB2(NT_run - 2)
        bk_tailB2(NT_run - 1)

    if stages < 2:
        dmaY(out=x0b[0][:], in_=x_in[0, 0:128, :], writes=XK(0))
        dmaY(out=out[0, 0:128, :], in_=x0b[0][:], reads=XK(0), writes=["o0"])
    if stages < 3:
        NT_p2 = 0
    else:
        NT_p2 = NT
    if stages >= 3:
        lgv = logit.rearrange("p (s e) -> p s e", s=NT)
        mxA = stats[:, 0:NT] if NT <= 64 else None
        exv = tA[:, 0:NL]
        exv3 = exv.rearrange("p (s e) -> p s e", s=NT)
        E_("V", "reduce_max", mxA.unsqueeze(-1), lgv, axis=AX.X,
           reads=["logit"], writes=["st"])
        E_("V", "tensor_tensor", exv3, lgv,
           mxA.unsqueeze(-1).broadcast_to((128, NT, NEXP)), op=ALU.subtract,
           reads=["logit", "st"], writes=["tA"])
        E_("S", "activation", exv, exv, AF.Exp, reads=["tA"], writes=["tA"])
        E_("V", "tensor_reduce", gateA.unsqueeze(-1), exv3, axis=AX.X, op=ALU.add,
           reads=["tA"], writes=["gateA"])
        E_("V", "reciprocal", gateA[:], gateA[:], reads=["gateA"], writes=["gateA"])
        E_("V", "tensor_tensor", exv3, lgv,
           mxA.unsqueeze(-1).broadcast_to((128, NT, NEXP)), op=ALU.is_equal,
           reads=["logit", "st"], writes=["tA"])
        iov = iota8.unsqueeze(1).broadcast_to((128, NT, NEXP))
        E_("V", "scalar_tensor_tensor", exv3, exv3, -999.0, iov,
           op0=ALU.mult, op1=ALU.add, reads=["tA", "io8"], writes=["tA"])
        E_("V", "tensor_reduce", bestA.unsqueeze(-1), exv3, axis=AX.X, op=ALU.min,
           reads=["tA"], writes=["bestA"])
        E_("V", "tensor_scalar", bestA[:], bestA[:], 999.0, None, op0=ALU.add,
           reads=["bestA"], writes=["bestA"])
        E_("V", "tensor_tensor", exv3, iov,
           bestA.unsqueeze(-1).broadcast_to((128, NT, NEXP)), op=ALU.is_equal,
           reads=["bestA", "io8"], writes=["tA"])
        mbv = ar1[:, 0:NL]
        E_("V", "tensor_copy", mbv, exv, reads=["tA"], writes=["ar1"])
        nsl = (NL + 127) // 128
        for c in range(nsl):
            w = min(128, NL - 128 * c)
            E_("T", "matmul", pA[0:w, c:c + 1], mbv[:, 128 * c:128 * c + w],
               o128b[:], start=True, stop=True,
               reads=["ar1", "o128"], writes=[pAk[0]])
        E_("V", "tensor_copy", stats[:, 0:nsl], pA[:, 0:nsl],
           reads=[pAk[0]], writes=["st"])
        E_("V", "tensor_copy", mk8b[:, 0:nsl], stats[:, 0:nsl],
           reads=["st"], writes=["mk8b"])
        for c in range(nsl):
            w = min(128, NL - 128 * c)
            E_("T", "matmul", pA[0:1, 8:16], mk8b[0:w, c:c + 1], sel8[0:w, :],
               start=(c == 0), stop=(c == nsl - 1),
               reads=["mk8b", "sel8"], writes=[pAk[0]])
        E_("V", "tensor_copy", allc[0:1, 0:NEXP], pA[0:1, 8:16],
           reads=[pAk[0]], writes=["allc"])
        E_("V", "tensor_copy", offs[0:1, 8:16], allc[0:1, 0:NEXP],
           reads=["allc"], writes=["mytot"])
        dmaG(out=cc_in[:, :], in_=offs[0:1, 8:16], reads=["mytot"], writes=["ccin"])
        E_("G", "collective_compute", "AllGather", ALU.bypass,
           replica_groups=[list(range(NCORES))], ins=[cc_in[:, :]],
           outs=[cc_out[:, :]], reads=["ccin"], writes=["ccout"])
        dmaG(out=allc[:], in_=cc_out[:, :], reads=["ccout"], writes=["allc"])
        E_("T", "matmul", pA[0:1, 24:32], prevm[:], allc[:],
           start=True, stop=True, reads=["allc", "prevm"], writes=[pAk[0]])
        E_("V", "tensor_copy", offs[0:1, 0:NEXP], pA[0:1, 24:32],
           reads=[pAk[0]], writes=["offs"])

        for r in range(2):
            dmaY(out=pool[BXP[r][0]:BXP[r][0] + 1, BXP[r][1]:BXP[r][1] + 512],
                 in_=bx2[r:r + 1, :], writes=[f"bx{r}", BXOVL[r]])
        for sl in range(8):
            dmaY(out=tA[:, 0:1024], in_=w_expert[128 * sl:128 * (sl + 1), :],
                 writes=["tA"])
            E_("S", "copy", out=wex[:, sl, 0:1024], in_=tA[:, 0:1024],
               reads=["tA", "tAh0", "tAh1"], writes=["wqkT"])

    def load_x1(s):
        if s >= NT_p2:
            return
        b, j = divmod(s, TJ)
        dmaG(out=x0b[s % 3][:], in_=x1s[b, 128 * j:128 * (j + 1), :],
             reads=[f"x1s.{s}"], writes=XK(s % 3))

    def p2_front(s):
        if s >= NT_p2:
            return
        p = s % 2
        dmaG(out=aTb[p][:, :, 3:131], in_=fTs[s, :, :],
             reads=[f"fTs.{s}"], writes=[f"aT{p}"])

    def p2_exp(s):
        p = s % 2
        off = 1024 * p
        pks = [pAk[2 * p], pAk[2 * p + 1]]
        for n in range(2):
            r0, c0 = BXP[n]
            E_("T", "matmul", pA[:, off + 512 * n:off + 512 * (n + 1)],
               pool[r0:r0 + 1, 0:128], pool[r0:r0 + 1, c0:c0 + 512],
               start=True, stop=False,
               reads=["ones", f"bx{n}"], writes=[pks[n]])
        for ch in range(8):
            for n in range(2):
                E_("T", "matmul", pA[:, off + 512 * n:off + 512 * (n + 1)],
                   aTb[p][:, ch, 3:131],
                   wex[:, ch, 512 * n:512 * (n + 1)],
                   start=False, stop=(ch == 7),
                   reads=[f"aT{p}", "wqkT"], writes=[pks[n]])
        E_("V", "tensor_scalar", mk8[:], iota8[:], bestA[:, s:s + 1], None,
           op0=ALU.is_equal, reads=["io8", "bestA"], writes=["mk8"])
        E_("V", "tensor_copy", mk8b[:], mk8[:], reads=["mk8"], writes=["mk8b"])
        E_("T", "matmul", pB[:, 1024:1024 + NEXP], onesr, offs[0:1, 0:NEXP],
           start=True, stop=False, reads=["ones", "offs"], writes=[pBk[2]])
        E_("T", "matmul", pB[:, 1024:1024 + NEXP], Lt[:], mk8b[:],
           start=False, stop=True, reads=["Lt", "mk8b"], writes=[pBk[2]])
        E_("T", "matmul", pB[0:1, 1536:1536 + NEXP], o128b[:], mk8b[:],
           start=True, stop=True, reads=["o128", "mk8b"], writes=[pBk[3]])
        E_("V", "tensor_add", offs[0:1, 0:NEXP], offs[0:1, 0:NEXP],
           pB[0:1, 1536:1536 + NEXP], reads=[pBk[3], "offs"], writes=["offs"])
        E_("V", "tensor_scalar", km8[:], pB[:, 1024:1024 + NEXP],
           CAP + 0.5, None, op0=ALU.is_lt, reads=[pBk[2]], writes=["km8"])
        E_("V", "tensor_mul", km8[:], km8[:], mk8[:],
           reads=["km8", "mk8"], writes=["km8"])
        keep = stats[:, 8:9]
        coef = stats[:, 9:10]
        E_("V", "tensor_reduce", keep.unsqueeze(-1), km8.unsqueeze(1),
           axis=AX.X, op=ALU.add, reads=["km8"], writes=["st"])
        E_("V", "tensor_mul", coef, keep, gateA[:, s:s + 1],
           reads=["st", "gateA"], writes=["st"])
        E_("S", "activation", tA[:, off:off + 1024], pA[:, off:off + 1024],
           AF.Identity, scale=coef, accum_out=stats[:, 1:2],
           reads=pks + ["st"], writes=[f"tAh{p}", "st"])

    def p2_fin(s):
        p = s % 2
        b, j = divmod(s, TJ)
        xk = XK(s % 3)
        x1a = x0b[s % 3]
        tAh = tA[:, 1024 * p:1024 * (p + 1)]
        E_("S", "activation", x1t2[1][:], tAh, AF.Square,
           accum_out=stats[:, 0:1], reads=[f"tAh{p}"], writes=["x1t1", "st"])
        rstd4, nb4 = ln_statsB(D, stats, "st")
        E_("V", "scalar_tensor_tensor", x1a[:], tAh, rstd4, x1a[:],
           op0=ALU.mult, op1=ALU.add, reads=[f"tAh{p}", "st"] + xk, writes=xk)
        E_("V", "tensor_scalar", x1a[:], x1a[:], nb4, None, op0=ALU.add,
           reads=xk + ["st"], writes=xk)
        dmaG(out=out[b, 128 * j:128 * (j + 1), :], in_=x1a[:],
             reads=xk, writes=[f"out.{s}"], qcls="st")

    if NT_p2 > 0:
        load_x1(0)
        load_x1(1)
        load_x1(2)
        p2_front(0)
    for s in range(NT_p2):
        p2_exp(s)
        p2_front(s + 1)
        p2_fin(s)
        load_x1(s + 3)

    import contextlib
    es = contextlib.ExitStack()
    semmap = {}
    for sname in sorted(set(list(em.tick) + list(em.qcnt))):
        semmap[sname] = es.enter_context(nc.semaphore(f"sm_{sname}"))
    block = es.enter_context(nc.Block())

    @block.sync
    def _(y):
        em.replay("Y", y, semmap)
        em.final_waits("Y", y, semmap)

    @block.tensor
    def _(t):
        em.replay("T", t, semmap)

    @block.vector
    def _(v):
        em.replay("V", v, semmap)

    @block.scalar
    def _(s):
        em.replay("S", s, semmap)

    @block.gpsimd
    def _(g):
        em.replay("G", g, semmap)

    es.close()
    return nc


def make_inputs(core_id, B, full):
    f32 = np.float32
    bfl = ml_dtypes.bfloat16
    T = full["L0_attn_x"].shape[1]
    TJ = T // 128
    x = np.ascontiguousarray(
        full["L0_attn_x"][core_id * B:(core_id + 1) * B]).astype(f32)
    convw = np.asarray(full["L0_attn_conv_w"]).reshape(D, 4)
    convw_r = np.ascontiguousarray(
        convw.reshape(8, 128, 4).transpose(1, 2, 0)).astype(f32)
    convb_r = np.ascontiguousarray(
        np.asarray(full["L0_attn_conv_b"]).reshape(8, 128).T).astype(f32)
    wgr = np.ascontiguousarray(np.asarray(full["L0_ffn_w_gate"]).reshape(
        8, 128, NEXP).transpose(1, 0, 2)).astype(f32)
    tc = np.arange(128, dtype=f32)[:, None] + 128.0 * np.arange(TJ, dtype=f32)[None, :]
    sel = np.zeros((128, NEXP), f32)
    sel[np.arange(128), np.arange(128) % NEXP] = 1.0
    pm = np.zeros((NCORES, 1), f32)
    pm[:core_id] = 1.0
    return {
        "x": x,
        "qk_w": np.asarray(full["L0_attn_qk_w"], dtype=f32),
        "v_w": np.asarray(full["L0_attn_v_w"], dtype=f32),
        "g_w": np.asarray(full["L0_attn_g_w"], dtype=f32),
        "o_w": np.asarray(full["L0_attn_o_w"], dtype=f32),
        "w_expert": np.asarray(full["L0_ffn_w_expert"], dtype=f32),
        "w_gate_r": wgr,
        "convw_r": convw_r,
        "convb_r": convb_r,
        "gb4": np.asarray(full["L0_attn_g_b"], dtype=f32).reshape(4, 512),
        "bx2": np.asarray(full["L0_ffn_b_expert"], dtype=f32).reshape(2, 512),
        "fq4": np.asarray(full["L0_attn_freqs"], dtype=f32).reshape(4, 512),
        "lng2": np.asarray(full["L0_attn_ln_g"], dtype=f32).reshape(2, 512),
        "lnb2": np.asarray(full["L0_attn_ln_b"], dtype=f32).reshape(2, 512),
        "gam16": np.asarray(full["L0_attn_gamma"], dtype=f32).reshape(1, H),
        "prevmask": pm,
        "I128": np.eye(128, dtype=bfl),
        "Ltri": np.triu(np.ones((128, 128), f32)).astype(bfl),
        "Ssh": np.eye(128, k=1, dtype=f32),
        "iota8": np.tile(np.arange(NEXP, dtype=f32), (128, 1)),
        "tcols": np.ascontiguousarray(tc),
        "ones128bf": np.ones((128, 1), dtype=bfl),
        "onesf32": np.ones((1, 128), dtype=f32),
        "sel8": np.ascontiguousarray(sel.astype(bfl)),
    }


_CACHE = {}


def kernel(**inputs):
    Bfull, T, _ = inputs["L0_attn_x"].shape
    B = Bfull // NCORES
    key = (B, T)
    if key not in _CACHE:
        _CACHE[key] = build_program(B, T)
    nc = _CACHE[key]
    in_maps = [make_inputs(c, B, inputs) for c in range(NCORES)]
    res = run_bass_kernel_spmd(nc, in_maps, core_ids=list(range(NCORES)))
    return np.concatenate([r["out"] for r in res.results], axis=0)
